# version 05
# baseline (speedup 1.0000x reference)
import numpy as np
from contextlib import ExitStack

import concourse.bass as bass
import concourse.mybir as mybir
from concourse.bass_utils import run_bass_kernel_spmd

F32 = mybir.dt.float32
I32 = mybir.dt.int32
AF = mybir.ActivationFunctionType
ALU = mybir.AluOpType
AX = mybir.AxisListType

S = 2048
D = 1024
NB = 4
BT = 512
NT = 4
ND = 8
NE = 16
NI = 32
NO = 8
NW2U = 32
EPS = 1e-5


class Ctr:

    def __init__(self):
        self.n = 0
        self.at = {}

    def ev(self, key, k=1):
        self.n += k
        self.at[key] = self.n
        return self.n


def plan():
    mm = Ctr()
    ve = Ctr()
    sc = Ctr()
    qx = Ctr()
    xr = Ctr()

    for b in range(NB):
        for k in range(NT):
            qx.ev(("store", b, k), 16)

    qwt = {}
    cnt = {}
    for b in range(NB):
        for name, n in (("whg", NE), ("w1", NI), ("w2", NW2U)):
            for u in range(n):
                key = (name, u % 2)
                cnt[key] = cnt.get(key, 0) + 1
                qwt[(name, b, u)] = 16 * cnt[key]

    def tr_events(b):
        for k in range(NT):
            for h in range(2):
                mm.ev(("tr", b, k, h))

    tr_events(0)
    for b in range(NB):
        for j in range(NE):
            mm.ev(("hg", b, j))
        for i in range(NI):
            mm.ev(("f1", b, i))
        for u in range(NW2U):
            mm.ev(("f2u", b, u))
        for k in range(NT):
            for h in range(2):
                mm.ev(("ytr", b, k, h))
        if b + 1 < NB:
            tr_events(b + 1)

    ve.ev(("init",))

    def ln_events(b):
        for k in range(NT):
            ve.ev(("xn", b, k))

    ln_events(0)
    for b in range(NB):
        for k in range(NT):
            for h in range(2):
                ve.ev(("evac", b, k, h))
        for kk in range(ND):
            ve.ev(("lin", b, kk))
            ve.ev(("scan", b, kk))
        if b + 1 < NB:
            ln_events(b + 1)
        for o in range(NO):
            ve.ev(("fb", b, o))
            ve.ev(("g2", b, o))
            ve.ev(("y", b, o))
        for k in range(NT):
            ve.ev(("res", b, k))

    for b in range(NB):
        for kk in range(ND):
            sc.ev(("z", b, kk))
            sc.ev(("sigh", b, kk))
        for i in range(NI):
            sc.ev(("h1", b, i))
        for o in range(NO):
            sc.ev(("sf", b, o))

    return mm, ve, sc, qwt, qx, xr


def build_nc():
    mm, ve, sc, qwt, qx, xr = plan()
    MM, VE, SC, QWT, QX, XR = mm.at, ve.at, sc.at, qwt, qx.at, xr.at

    nc = bass.Bass("TRN2", target_bir_lowering=False,
                   detect_race_conditions=False)

    x_hbm = nc.dram_tensor("x", [S, D], F32, kind="ExternalInput").ap()
    whg_hbm = nc.dram_tensor("whg", [NE, 128, ND * 128], F32, kind="ExternalInput").ap()
    w1_hbm = nc.dram_tensor("w1", [NI, 128, ND * 128], F32, kind="ExternalInput").ap()
    w2_hbm = nc.dram_tensor("w2", [NW2U, 128, 8 * 128], F32, kind="ExternalInput").ap()
    b1_hbm = nc.dram_tensor("b1t", [128, NI], F32, kind="ExternalInput").ap()
    b2_hbm = nc.dram_tensor("b2t", [128, NO], F32, kind="ExternalInput").ap()
    id_hbm = nc.dram_tensor("ident", [128, 128], F32, kind="ExternalInput").ap()
    out_hbm = nc.dram_tensor("out", [S, D], F32, kind="ExternalOutput").ap()

    es = ExitStack()
    with es:
        def sb(name, shape, dt=F32):
            return es.enter_context(nc.sbuf_tensor(name, shape, dt))

        def ps(name, shape):
            return es.enter_context(nc.psum_tensor(name, shape, F32))

        xt = [[sb(f"xt{p}_{k}", [128, D]) for k in range(NT)] for p in range(2)]
        xn_tm = [sb(f"xntm{k}", [128, D]) for k in range(NT)]
        xn = [sb(f"xn{d}", [128, BT]) for d in range(ND)]
        h1 = [sb(f"h1_{i}", [128, BT]) for i in range(NI)]
        s_t = [sb(f"s{d}", [128, BT]) for d in range(ND)]
        zbuf = [sb(f"z{p}", [128, BT]) for p in range(2)]
        gbuf = [sb(f"g{p}", [128, BT]) for p in range(2)]
        lbuf = sb("lbuf", [128, BT])
        vbuf = sb("vbuf", [128, BT])
        sfb = [sb(f"sf{p}", [128, BT]) for p in range(2)]
        g2b = [sb(f"g2{p}", [128, BT]) for p in range(2)]
        whg_b = [sb(f"whgb{p}", [128, ND * 128]) for p in range(2)]
        w1_b = [sb(f"w1b{p}", [128, ND * 128]) for p in range(2)]
        w2_b = [sb(f"w2b{p}", [128, 8 * 128]) for p in range(2)]
        stats = sb("stats", [128, NT * 8])
        carry = sb("carry", [128, ND])
        neghalf = sb("neghalf", [128, 1])
        b1t = sb("b1t_s", [128, NI])
        b2t = sb("b2t_s", [128, NO])
        ident = sb("ident_s", [128, 128])

        pm = [ps("pm0", [128, BT]), ps("pm1", [128, BT])]
        pt = [ps("pt0", [128, BT]), ps("pt1", [128, BT])]
        py = [ps("py0", [128, BT]), ps("py1", [128, BT])]

        s_mm = es.enter_context(nc.semaphore("s_mm"))
        s_ve = es.enter_context(nc.semaphore("s_ve"))
        s_sc = es.enter_context(nc.semaphore("s_sc"))
        q_x = es.enter_context(nc.semaphore("q_x"))
        q_c = es.enter_context(nc.semaphore("q_c"))
        q_xk = [es.enter_context(nc.semaphore(f"q_xk{k}")) for k in range(NT)]
        q_w = {
            name: [es.enter_context(nc.semaphore(f"qw_{name}{p}")) for p in range(2)]
            for name in ("whg", "w1", "w2")
        }
        block = es.enter_context(nc.Block())

        def hg_free_waits(eng, b, j):
            kk = j // 2
            if j % 2 == 0:
                eng.wait_ge(s_sc, SC[("z", b, kk)])
            else:
                eng.wait_ge(s_ve, VE[("lin", b, kk)])

        def f1_free_waits(eng, b, i):
            eng.wait_ge(s_sc, SC[("h1", b, i)])

        def f2_free_waits(eng, b, o):
            eng.wait_ge(s_ve, VE[("fb", b, o)])

        @block.sync
        def _(sync):
            hist = {"whg": [], "w1": [], "w2": []}

            def stream(name, b, u, src, buf2):
                lst = hist[name]
                if len(lst) >= 2:
                    pb, pu = lst[-2]
                    key = {"whg": "hg", "w1": "f1", "w2": "f2u"}[name]
                    sync.wait_ge(s_mm, MM[(key, pb, pu)])
                sync.dma_start(out=buf2[u % 2][:], in_=src).then_inc(
                    q_w[name][u % 2], 16
                )
                lst.append((b, u))

            for b in range(NB):
                for j in range(NE):
                    stream("whg", b, j, whg_hbm[j], whg_b)
                for i in range(NI):
                    stream("w1", b, i, w1_hbm[i], w1_b)
                for u in range(NW2U):
                    stream("w2", b, u, w2_hbm[u], w2_b)

        @block.gpsimd
        def _(gpsimd):
            gpsimd.dma_start(out=ident[:], in_=id_hbm[:]).then_inc(q_c, 16)
            gpsimd.dma_start(out=b1t[:], in_=b1_hbm[:]).then_inc(q_c, 16)
            gpsimd.dma_start(out=b2t[:], in_=b2_hbm[:]).then_inc(q_c, 16)

            def xload(b, k):
                t0 = b * BT + k * 128
                gpsimd.dma_start(
                    out=xt[b % 2][k][:], in_=x_hbm[t0:t0 + 128, :]
                ).then_inc(q_xk[k], 16)

            for b in (0, 1):
                for k in range(NT):
                    xload(b, k)
            for b in range(NB):
                for k in range(NT):
                    t0 = b * BT + k * 128
                    gpsimd.wait_ge(s_ve, VE[("res", b, k)])
                    gpsimd.dma_start(
                        out=out_hbm[t0:t0 + 128, :], in_=xt[b % 2][k][:]
                    ).then_inc(q_x, 16)
                if b + 2 < NB:
                    for k in range(NT):
                        xload(b + 2, k)
            gpsimd.wait_ge(q_x, qx.n)

        @block.tensor
        def _(tensor):
            tensor.wait_ge(q_c, 48)

            def tr_group(b):
                for k in range(NT):
                    tensor.wait_ge(s_ve, VE[("xn", b, k)])
                    for h in range(2):
                        if k >= 1:
                            tensor.wait_ge(s_ve, VE[("evac", b, k - 1, h)])
                        elif b >= 1:
                            tensor.wait_ge(s_ve, VE[("evac", b - 1, NT - 1, h)])
                        last = None
                        for dd in range(4):
                            d = h * 4 + dd
                            last = tensor.transpose(
                                pt[h][:, dd * 128:(dd + 1) * 128],
                                xn_tm[k][:, d * 128:(d + 1) * 128],
                                ident[:],
                            )
                        last.then_inc(s_mm, 1)

            def grp_free(eng, b, seq):
                p = seq - 2
                if p < 0:
                    return
                if p < NE:
                    hg_free_waits(eng, b, p)
                elif p < NE + NI:
                    f1_free_waits(eng, b, p - NE)
                else:
                    f2_free_waits(eng, b, p - NE - NI)

            tr_group(0)
            for b in range(NB):
                seq = 0
                for j in range(NE):
                    tensor.wait_ge(q_w["whg"][j % 2], QWT[("whg", b, j)])
                    if j == 0:
                        tensor.wait_ge(s_ve, VE[("evac", b, NT - 1, 1)])
                    grp_free(tensor, b, seq)
                    bank = pm[seq % 2]
                    last = None
                    for d in range(ND):
                        last = tensor.matmul(
                            bank[:], whg_b[j % 2][:, d * 128:(d + 1) * 128],
                            xn[d][:], start=(d == 0), stop=(d == ND - 1),
                        )
                    last.then_inc(s_mm, 1)
                    seq += 1
                for i in range(NI):
                    tensor.wait_ge(q_w["w1"][i % 2], QWT[("w1", b, i)])
                    grp_free(tensor, b, seq)
                    bank = pm[seq % 2]
                    last = None
                    for d in range(ND):
                        last = tensor.matmul(
                            bank[:], w1_b[i % 2][:, d * 128:(d + 1) * 128],
                            xn[d][:], start=(d == 0), stop=(d == ND - 1),
                        )
                    last.then_inc(s_mm, 1)
                    seq += 1
                for o in range(NO):
                    grp_free(tensor, b, seq)
                    bank = pm[seq % 2]
                    for u4 in range(4):
                        u = o * 4 + u4
                        tensor.wait_ge(q_w["w2"][u % 2], QWT[("w2", b, u)])
                        tensor.wait_ge(s_sc, SC[("h1", b, u4 * 8 + 7)])
                        last = None
                        for ii in range(8):
                            i = u4 * 8 + ii
                            last = tensor.matmul(
                                bank[:], w2_b[u % 2][:, ii * 128:(ii + 1) * 128],
                                h1[i][:], start=(i == 0), stop=(i == NI - 1),
                            )
                        last.then_inc(s_mm, 1)
                    seq += 1
                for k in range(NT):
                    for h in range(2):
                        tensor.wait_ge(s_ve, VE[("y", b, h * 4 + 3)])
                        if k >= 1:
                            tensor.wait_ge(s_ve, VE[("res", b, k - 1)])
                        elif b >= 1:
                            tensor.wait_ge(s_ve, VE[("res", b - 1, NT - 1)])
                        last = None
                        for dd in range(4):
                            d = h * 4 + dd
                            last = tensor.transpose(
                                py[h][:, dd * 128:(dd + 1) * 128],
                                s_t[d][:, k * 128:(k + 1) * 128],
                                ident[:],
                            )
                        last.then_inc(s_mm, 1)
                if b + 1 < NB:
                    tr_group(b + 1)

        @block.scalar
        def _(scalar):
            scalar.wait_ge(q_c, 48)
            for b in range(NB):
                for kk in range(ND):
                    if kk >= 2:
                        scalar.wait_ge(s_ve, VE[("scan", b, kk - 2)])
                    elif b > 0:
                        scalar.wait_ge(s_ve, VE[("scan", b - 1, kk + 6)])
                    scalar.wait_ge(s_mm, MM[("hg", b, 2 * kk)])
                    gate_bank = pm[(2 * kk) % 2]
                    scalar.activation(
                        out=zbuf[kk % 2][:], in_=gate_bank[:], func=AF.Sigmoid,
                    ).then_inc(s_sc, 1)
                    scalar.wait_ge(s_ve, VE[("lin", b, kk)])
                    scalar.activation(
                        out=gbuf[kk % 2][:], in_=lbuf[:], func=AF.Sigmoid,
                        bias=neghalf[:], scale=1.0,
                    ).then_inc(s_sc, 1)
                for i in range(NI):
                    scalar.wait_ge(s_mm, MM[("f1", b, i)])
                    bank = pm[(NE + i) % 2]
                    scalar.activation(
                        out=h1[i][:], in_=bank[:], func=AF.Relu,
                        bias=b1t[:, i:i + 1], scale=1.0,
                    ).then_inc(s_sc, 1)
                for o in range(NO):
                    if o >= 2:
                        scalar.wait_ge(s_ve, VE[("g2", b, o - 2)])
                    elif b > 0:
                        scalar.wait_ge(s_ve, VE[("g2", b - 1, o + 6)])
                    scalar.wait_ge(s_ve, VE[("fb", b, o)])
                    scalar.activation(
                        out=sfb[o % 2][:], in_=g2b[o % 2][:], func=AF.Sigmoid,
                    ).then_inc(s_sc, 1)

        @block.vector
        def _(vector):
            def vdrain():
                vector.drain()

            vector.memset(carry[:], 0.0)
            vector.memset(neghalf[:], -0.5).then_inc(s_ve, 1)

            def ln_vector(b):
                for k in range(NT):
                    vector.wait_ge(q_xk[k], 16 * (b + 1))
                    if b >= 1:
                        vector.wait_ge(s_mm, MM[("tr", b - 1, k, 1)])
                    base = k * 8
                    su = stats[:, base + 0:base + 1]
                    sq_ = stats[:, base + 1:base + 2]
                    mean = stats[:, base + 2:base + 3]
                    ex2 = stats[:, base + 3:base + 4]
                    w = stats[:, base + 4:base + 5]
                    rstd = stats[:, base + 6:base + 7]
                    tmp = stats[:, base + 7:base + 8]
                    vector.reduce_sum(su, xt[b % 2][k][:], axis=AX.X)
                    vector.scalar_tensor_tensor(
                        xn_tm[k][:], xt[b % 2][k][:], 1.0, xt[b % 2][k][:],
                        ALU.mult, ALU.mult, accum_out=sq_,
                    )
                    vdrain()
                    vector.tensor_scalar_mul(mean, su, 1.0 / D)
                    vector.tensor_scalar_mul(ex2, sq_, 1.0 / D)
                    vdrain()
                    vector.tensor_tensor(tmp, mean, mean, ALU.mult)
                    vdrain()
                    vector.tensor_sub(w, ex2, tmp)
                    vdrain()
                    vector.tensor_scalar_add(w, w, EPS)
                    vdrain()
                    vector.tensor_single_scalar(
                        rstd.bitcast(I32), w.bitcast(I32), 1,
                        ALU.logical_shift_right,
                    )
                    vdrain()
                    vector.tensor_scalar(
                        rstd.bitcast(I32), rstd.bitcast(I32), -1, 0x5F3759DF,
                        ALU.mult, ALU.add,
                    )
                    vdrain()
                    for _ in range(5):
                        vector.tensor_mul(tmp, w, rstd)
                        vdrain()
                        vector.tensor_mul(tmp, tmp, rstd)
                        vdrain()
                        vector.tensor_scalar(tmp, tmp, -0.5, 1.5, ALU.mult, ALU.add)
                        vdrain()
                        vector.tensor_mul(rstd, rstd, tmp)
                        vdrain()
                    vector.tensor_scalar(
                        xn_tm[k][:], xt[b % 2][k][:], mean, rstd,
                        ALU.subtract, ALU.mult,
                    ).then_inc(s_ve, 1)

            ln_vector(0)
            for b in range(NB):
                for k in range(NT):
                    for h in range(2):
                        vector.wait_ge(s_mm, MM[("tr", b, k, h)])
                        last = None
                        for dd in range(4):
                            d = h * 4 + dd
                            last = vector.tensor_copy(
                                xn[d][:, k * 128:(k + 1) * 128],
                                pt[h][:, dd * 128:(dd + 1) * 128],
                            )
                        last.then_inc(s_ve, 1)
                for kk in range(ND):
                    if b > 0 and kk == 0:
                        vector.wait_ge(s_mm, MM[("ytr", b - 1, NT - 1, 1)])
                    vector.wait_ge(s_mm, MM[("hg", b, 2 * kk + 1)])
                    hid_bank = pm[(2 * kk + 1) % 2]
                    vector.tensor_scalar_add(
                        lbuf[:], hid_bank[:], 0.5
                    ).then_inc(s_ve, 1)
                    vector.wait_ge(s_sc, SC[("sigh", b, kk)])
                    vdrain()
                    vector.tensor_max(lbuf[:], lbuf[:], gbuf[kk % 2][:])
                    vector.wait_ge(s_sc, SC[("z", b, kk)])
                    vdrain()
                    vector.tensor_mul(vbuf[:], zbuf[kk % 2][:], lbuf[:])
                    vector.tensor_scalar(
                        zbuf[kk % 2][:], zbuf[kk % 2][:], -1.0, 1.0, ALU.mult, ALU.add
                    )
                    vdrain()
                    vector.tensor_tensor_scan(
                        s_t[kk][:], zbuf[kk % 2][:], vbuf[:],
                        carry[:, kk:kk + 1], ALU.mult, ALU.add,
                    )
                    vdrain()
                    vector.tensor_copy(
                        carry[:, kk:kk + 1], s_t[kk][:, BT - 1:BT]
                    ).then_inc(s_ve, 1)
                if b + 1 < NB:
                    ln_vector(b + 1)
                for o in range(NO):
                    vector.wait_ge(s_mm, MM[("f2u", b, o * 4 + 3)])
                    bank = pm[(NE + NI + o) % 2]
                    vector.tensor_scalar_add(
                        g2b[o % 2][:], bank[:], b2t[:, o:o + 1]
                    ).then_inc(s_ve, 1)
                    vector.wait_ge(s_sc, SC[("sf", b, o)])
                    vdrain()
                    vector.tensor_mul(
                        g2b[o % 2][:], g2b[o % 2][:], sfb[o % 2][:]
                    ).then_inc(s_ve, 1)
                    vdrain()
                    vector.tensor_mul(
                        s_t[o][:], s_t[o][:], g2b[o % 2][:]
                    ).then_inc(s_ve, 1)
                for k in range(NT):
                    last = None
                    for h in range(2):
                        vector.wait_ge(s_mm, MM[("ytr", b, k, h)])
                        last = vector.tensor_tensor(
                            xt[b % 2][k][:, h * 512:(h + 1) * 512],
                            xt[b % 2][k][:, h * 512:(h + 1) * 512],
                            py[h][:], ALU.add,
                        )
                    last.then_inc(s_ve, 1)

    return nc


_NC_CACHE = None


def kernel(L0_attn_x, L0_attn_W_hg, L0_ffn_W1, L0_ffn_b1, L0_ffn_W2, L0_ffn_b2):
    global _NC_CACHE
    x = np.ascontiguousarray(np.asarray(L0_attn_x, dtype=np.float32))
    whg = np.asarray(L0_attn_W_hg, dtype=np.float32)
    w1 = np.asarray(L0_ffn_W1, dtype=np.float32)
    b1 = np.asarray(L0_ffn_b1, dtype=np.float32)
    w2 = np.asarray(L0_ffn_W2, dtype=np.float32)
    b2 = np.asarray(L0_ffn_b2, dtype=np.float32)

    e_order = []
    for kk in range(8):
        e_order.extend([8 + kk, kk])
    whg_t = whg.reshape(8, 128, 16, 128).transpose(2, 1, 0, 3)[e_order]
    whg_t = np.ascontiguousarray(whg_t.reshape(16, 128, 1024))
    w1_t = np.ascontiguousarray(
        w1.reshape(8, 128, 32, 128).transpose(2, 1, 0, 3).reshape(32, 128, 1024)
    )
    w2_t = np.ascontiguousarray(
        w2.reshape(4, 8, 128, 8, 128).transpose(3, 0, 2, 1, 4).reshape(32, 128, 1024)
    )
    b1_t = np.ascontiguousarray(b1.reshape(32, 128).T)
    b2_t = np.ascontiguousarray(b2.reshape(8, 128).T)
    ident = np.eye(128, dtype=np.float32)

    if _NC_CACHE is None:
        _NC_CACHE = build_nc()
    nc = _NC_CACHE

    in_maps = [
        {
            "x": np.ascontiguousarray(x[c]),
            "whg": whg_t,
            "w1": w1_t,
            "w2": w2_t,
            "b1t": b1_t,
            "b2t": b2_t,
            "ident": ident,
        }
        for c in range(8)
    ]
    res = run_bass_kernel_spmd(nc, in_maps, core_ids=list(range(8)))
    return np.stack([r["out"] for r in res.results], axis=0)
